# version 13 (final)
# speedup vs baseline: 1.7053x; 1.0000x over previous
import numpy as np
import concourse.bass as bass
import concourse.mybir as mybir
from concourse.bass import IndirectOffsetOnAxis
from concourse.bass_utils import run_bass_kernel_spmd

F32 = mybir.dt.float32
BF16 = mybir.dt.bfloat16
U32 = mybir.dt.uint32
AF = mybir.ActivationFunctionType
ALU = mybir.AluOpType
AX = mybir.AxisListType

C = 1024
HEADS = 16
T_ALL = 16384
T_OWN = 2048
T_BATCH = 4096
NEXP = 8
DFF = 4096
CAP = 384
NBIS = 27
BIG = 4000000
KTOP = 2048.0


class Sched:
    ENG = ("sync", "vector", "scalar", "tensor", "gpsimd")

    def __init__(self, nc):
        self.nc = nc
        self.streams = {e: [] for e in self.ENG}
        self.sems = {}
        self.counts = {}
        self.owner = {}
        self.known = {e: {} for e in self.ENG}
        self.ninstr = 0

    def add_sem(self, name, handle, owner=""):
        self.sems[name] = handle
        self.counts[name] = 0
        self.owner[name] = owner

    AUTOFLUSH = {"vector": "DV", "scalar": "SC", "gpsimd": "PL"}

    def op(self, eng, fn, wait=(), inc=(), flush=False):
        wl = []
        own = self.AUTOFLUSH.get(eng)
        if flush and own is not None and self.counts.get(own, 0) > 0:
            wl.append((own, self.counts[own]))
            self.known[eng][own] = self.counts[own]
        for w in wait:
            if w is None:
                continue
            if isinstance(w, str):
                w = (w, self.counts[w])
            s, v = w
            if v <= 0:
                continue
            if self.owner[s] == eng and eng == "tensor" and not flush:
                continue
            if self.owner[s] == "":
                v = self.counts[s]
            if not flush and self.known[eng].get(s, 0) >= v:
                continue
            self.known[eng][s] = v
            wl.append((s, v))
        il = []
        for s, n in inc:
            self.counts[s] += n
            il.append((s, n))
            if self.owner[s] == eng and eng == "tensor":
                self.known[eng][s] = self.counts[s]
        tok = (il[0][0], self.counts[il[0][0]]) if il else None
        sems = self.sems

        import traceback as _tb
        _site = "".join(_tb.format_stack(limit=6)[:-2])

        def replay(engobj, wl=wl, il=il, fn=fn, _site=_site):
            for s, v in wl:
                engobj.wait_ge(sems[s], v)
            try:
                instr = fn(engobj)
            except Exception:
                print("SCHED SITE:\n", _site)
                raise
            for s, n in il:
                instr.then_inc(sems[s], n)
        self.streams[eng].append(replay)
        self.ninstr += 1 + len(wl)
        return tok

    def waitonly(self, eng, wait):
        sems = self.sems

        def replay(engobj, wait=list(wait)):
            for s, v in wait:
                engobj.wait_ge(sems[s], v)
        self.streams[eng].append(replay)


def build(debug_taps=(), last_phase=99, no_prefetch=False):
    nc = bass.Bass("TRN2", target_bir_lowering=False)
    ap = lambda t: t.ap()
    X = ap(nc.dram_tensor("X", [T_ALL, C], F32, kind="ExternalInput"))
    WQ1 = ap(nc.dram_tensor("WQ1", [C, C], F32, kind="ExternalInput"))
    WQ2 = ap(nc.dram_tensor("WQ2", [C, C], F32, kind="ExternalInput"))
    WK1 = ap(nc.dram_tensor("WK1", [C, C], F32, kind="ExternalInput"))
    WK2 = ap(nc.dram_tensor("WK2", [C, C], F32, kind="ExternalInput"))
    WV = ap(nc.dram_tensor("WV", [C, C], F32, kind="ExternalInput"))
    WO = ap(nc.dram_tensor("WO", [C, C], F32, kind="ExternalInput"))
    BIAS6 = ap(nc.dram_tensor("BIAS6", [8, C], F32, kind="ExternalInput"))
    BO = ap(nc.dram_tensor("BO", [C], F32, kind="ExternalInput"))
    WR = ap(nc.dram_tensor("WR", [C, NEXP], F32, kind="ExternalInput"))
    BR = ap(nc.dram_tensor("BR", [NEXP], F32, kind="ExternalInput"))
    FW1 = ap(nc.dram_tensor("FW1", [NEXP, C, DFF], F32, kind="ExternalInput"))
    FB1 = ap(nc.dram_tensor("FB1", [NEXP, DFF], F32, kind="ExternalInput"))
    FW2 = ap(nc.dram_tensor("FW2", [NEXP, DFF, C], F32, kind="ExternalInput"))
    FB2 = ap(nc.dram_tensor("FB2", [NEXP, C], F32, kind="ExternalInput"))
    CIDN = ap(nc.dram_tensor("CIDN", [128, 128], F32, kind="ExternalInput"))
    CSLT = ap(nc.dram_tensor("CSLT", [128, 128], F32, kind="ExternalInput"))
    OUT = ap(nc.dram_tensor("OUT", [T_OWN, C], F32, kind="ExternalOutput"))

    dram = lambda n, s, d: ap(nc.dram_tensor(n, s, d, kind="Internal"))
    d_xn = dram("d_xn", [T_OWN, C], BF16)
    d_q1 = dram("d_q1", [C, T_OWN], BF16)
    d_q2 = dram("d_q2", [C, T_OWN], BF16)
    d_k1 = dram("d_k1", [C, T_BATCH], BF16)
    d_k2 = dram("d_k2", [C, T_BATCH], BF16)
    d_v = dram("d_v", [C, T_BATCH], BF16)
    d_tok = [dram(f"d_tok{i}", [CAP, 2], U32) for i in range(NEXP)]
    d_ctr = [dram(f"d_ctr{i}", [CAP, C], F32) for i in range(NEXP)]
    d_g = dram("d_g", [T_OWN, C], BF16)
    d_b1 = dram("d_b1", [128, NEXP], F32)
    d_b2 = dram("d_b2", [NEXP], F32)
    d_u = dram("d_u", [C], F32)
    d_c0 = dram("d_c0", [C], F32)
    d_dbg = {}
    for name, shape in debug_taps:
        dt_ = BF16 if name.endswith("_bf") else F32
        d_dbg[name] = ap(nc.dram_tensor("dbg_" + name, list(shape), dt_,
                                        kind="ExternalOutput"))

    from contextlib import ExitStack
    ctx = ExitStack()
    E = ctx.enter_context

    AR = E(nc.sbuf_tensor("AR", [128, 90112], BF16))
    ARf = AR[:].bitcast(F32)

    def arf(off_kib, cols):
        return ARf[:, off_kib * 256: off_kib * 256 + cols]

    def arb(off_kib, cols):
        return AR[:, off_kib * 512: off_kib * 512 + cols]

    c_idn = E(nc.sbuf_tensor("c_idn", [128, 128], BF16))
    c_idf = E(nc.sbuf_tensor("c_idf", [128, 128], F32))
    c_slt = E(nc.sbuf_tensor("c_slt", [128, 128], BF16))
    c_one = E(nc.sbuf_tensor("c_one", [128, 1], BF16))
    c_os = E(nc.sbuf_tensor("c_os", [128, 128], BF16))
    stg128 = E(nc.sbuf_tensor("stg128", [128, 128], F32))
    scal = E(nc.sbuf_tensor("scal", [128, 1024], F32))
    mask8 = E(nc.sbuf_tensor("mask8", [128, 128], F32))
    slot8 = E(nc.sbuf_tensor("slot8", [128, 128], U32))
    slotf = E(nc.sbuf_tensor("slotf", [128, 16], F32))
    gmask8 = E(nc.sbuf_tensor("gmask8", [128, 128], F32))
    battn = E(nc.sbuf_tensor("battn", [128, 64], F32))
    lnbb = E(nc.sbuf_tensor("lnbb", [128, 8], BF16))
    lngb = E(nc.sbuf_tensor("lngb", [128, 8], F32))
    wrb = E(nc.sbuf_tensor("wrb", [128, 64], BF16))
    b1c = E(nc.sbuf_tensor("b1c", [128, 32], F32))
    bis = E(nc.sbuf_tensor("bis", [128, 16], F32))
    bn12 = E(nc.sbuf_tensor("bn12", [128, 48], F32))
    mvst = E(nc.sbuf_tensor("mvst", [128, 32], F32))
    stile = E(nc.sbuf_tensor("stile", [128, 32], F32))
    l8 = E(nc.sbuf_tensor("l8", [128, 8], F32))
    iotac = E(nc.sbuf_tensor("iotac", [128, 32], U32))
    toks = E(nc.sbuf_tensor("toks", [128, 3], U32))
    tmpb = E(nc.sbuf_tensor("tmpb", [128, 4], BF16))
    bigf = E(nc.sbuf_tensor("bigf", [128, 16], F32))
    sc16 = E(nc.sbuf_tensor("sc16", [128, 48], F32))
    panel = E(nc.sbuf_tensor("panel", [128, 32], F32))
    ostat = E(nc.sbuf_tensor("ostat", [128, 96], F32))
    brb = E(nc.sbuf_tensor("brb", [128, 8], F32))
    oshift = E(nc.sbuf_tensor("oshift", [64, 256], BF16))
    ARu = AR[:].bitcast(U32)
    bisr = ARf[0:1, 122 * 256:122 * 256 + 1152]
    bigr = ARu[0:1, 128 * 256:128 * 256 + 3072]
    orow = ARf[0:1, 106 * 256:106 * 256 + 2048]
    bo_row = ARf[0:1, 114 * 256:114 * 256 + 1024]
    ubc = arf(98, 1024)
    c0bc = arf(102, 1024)

    PA0 = E(nc.psum_tensor("PA0", [128, 512], F32))
    PA1 = E(nc.psum_tensor("PA1", [128, 512], F32))
    PB0 = E(nc.psum_tensor("PB0", [128, 512], F32))
    PB1 = E(nc.psum_tensor("PB1", [128, 512], F32))
    PT0 = E(nc.psum_tensor("PT0", [128, 1024], BF16))
    PT1 = E(nc.psum_tensor("PT1", [128, 1024], BF16))
    PS = E(nc.psum_tensor("PS", [128, 512], F32))
    PSB = E(nc.psum_tensor("PSB", [128, 1024], BF16))

    S = Sched(nc)
    for name, owner in [("QS", ""), ("QW", ""), ("QG", ""), ("QH0", ""),
                        ("QH1", ""), ("QO", ""), ("QX0", ""), ("QX1", ""),
                        ("QX2", ""), ("QX3", ""), ("QN2", ""), ("QN3", ""),
                        ("QN0", ""), ("QN1", ""), ("QWa", ""), ("QWb", ""),
                        ("QW2a", ""), ("QW2b", ""), ("QWc", ""), ("QWd", ""),
                        ("QW2c", ""), ("QW2d", ""), ("QWe", ""), ("QWf", ""),
                        ("QP0", ""), ("QP1", ""),
                        ("QC0", ""), ("QC1", ""), ("QD0", ""), ("QD1", ""),
                        ("QGA0", ""), ("QGA1", ""), ("QGA2", ""), ("QGA3", ""),
                        ("QW6a", ""), ("QW6b", ""),
                        ("PE", "tensor"),
                        ("DV", "vector"), ("SC", "scalar"), ("PL", "gpsimd")]:
        S.add_sem(name, E(nc.semaphore(name)), owner)

    regh = {}

    def get_breg(val):
        def fn(e, val=val):
            if val not in regh:
                regh[val] = e.alloc_register()
            return regh[val]
        return fn

    def dma(queue, sem, out, in_, wait=(), **kw):
        return S.op(queue, lambda e, o=out, i=in_, kw=kw: e.dma_start(out=o, in_=i, **kw),
                    wait=wait, inc=[(sem, 16)], flush=True)

    bregs = {}

    def breg(e, val):
        if val not in bregs:
            r = e.alloc_register()
            e.reg_mov(r, val)
            bregs[val] = r
        return bregs[val]

    def idma_g(out, in_, idx, bounds, wait=(), eoff=0):
        return S.op("gpsimd",
                    lambda e, o=out, i=in_, x=idx, b=bounds, f=eoff:
                    e.indirect_dma_start(out=o, out_offset=None, in_=i,
                                         in_offset=IndirectOffsetOnAxis(ap=x, axis=0),
                                         element_offset=f, bounds_check=breg(e, b),
                                         oob_is_err=False),
                    wait=wait, inc=[("QG", 16)], flush=True)

    def idma_g2(out, in_, idx, bounds, sem, wait=(), eoff=0):
        return S.op("gpsimd",
                    lambda e, o=out, i=in_, x=idx, b=bounds, f=eoff:
                    e.indirect_dma_start(out=o, out_offset=None, in_=i,
                                         in_offset=IndirectOffsetOnAxis(ap=x, axis=0),
                                         element_offset=f, bounds_check=breg(e, b),
                                         oob_is_err=False),
                    wait=wait, inc=[(sem, 16)], flush=True)

    def idma_s(out, in_, idx, bounds, wait=()):
        return S.op("gpsimd",
                    lambda e, o=out, i=in_, x=idx, b=bounds:
                    e.indirect_dma_start(out=o, out_offset=IndirectOffsetOnAxis(ap=x, axis=0),
                                         in_=i, in_offset=None,
                                         bounds_check=breg(e, b), oob_is_err=False),
                    wait=wait, inc=[("QG", 16)], flush=True)

    def V(fn, wait=(), flush=False):
        return S.op("vector", fn, wait=wait, inc=[("DV", 1)], flush=flush)

    def SCE(fn, wait=(), flush=False):
        return S.op("scalar", fn, wait=wait, inc=[("SC", 1)], flush=flush)

    def PE(fn, wait=(), flush=False):
        return S.op("tensor", fn, wait=wait, inc=[("PE", 1)], flush=flush)

    def PL(fn, wait=(), flush=False):
        return S.op("gpsimd", fn, wait=wait, inc=[("PL", 1)], flush=flush)

    def tap(name, src_ap):
        if name in d_dbg:
            dma("sync", "QO", d_dbg[name][:, :], src_ap,
                wait=["DV", "SC", "PE", "PL", "QG", "QS"])


    def finish_early():
        zt = V(lambda e: e.memset(stg128[:], 0.0),
               wait=["PE", "SC", "PL", "QG", "QS", "QW", "QO", "QH0", "QH1"])
        for tt in range(16):
            for ccx in range(8):
                dma("sync", "QO",
                    OUT[128 * tt:128 * (tt + 1), 128 * ccx:128 * (ccx + 1)],
                    stg128[:], wait=[zt])
        S.waitonly("sync", [(s, S.counts[s]) for s in S.counts
                            if S.owner[s] == "" and S.counts[s] > 0])
        emit()
        ctx.close()
        return nc, S.ninstr

    def emit():
        import sys as _sys
        print("EMIT stream lens:", {k: len(v) for k, v in S.streams.items()},
              file=_sys.stderr)
        with nc.Block() as blk:
            @blk.sync
            def _(eng):
                for f in S.streams["sync"]:
                    f(eng)

            @blk.vector
            def _(eng):
                for f in S.streams["vector"]:
                    f(eng)

            @blk.scalar
            def _(eng):
                for f in S.streams["scalar"]:
                    f(eng)

            @blk.tensor
            def _(eng):
                for f in S.streams["tensor"]:
                    f(eng)

            @blk.gpsimd
            def _(eng):
                for f in S.streams["gpsimd"]:
                    f(eng)

    t_idn = dma("sync", "QS", c_idf[:], CIDN)
    ti = V(lambda e: e.tensor_copy(out=c_idn[:], in_=c_idf[:]), wait=[t_idn])
    t_slt = dma("sync", "QS", stg128[:], CSLT, wait=[ti])
    V(lambda e: e.tensor_copy(out=c_slt[:], in_=stg128[:]), wait=[t_slt])
    V(lambda e: e.memset(c_one[:], 1.0))
    V(lambda e: e.memset(bigf[:], float(BIG)))
    V(lambda e: e.memset(c_os[:], 1.0))
    for _t in (mvst, bn12, panel, ostat, bis, sc16, l8, stile, slotf, mask8,
               gmask8, b1c, battn):
        V(lambda e, tt=_t: e.memset(tt[:], 0.0))
    PL(lambda e: e.iota(out=iotac[:].rearrange("p (w x) -> p w x", x=2),
                        pattern=[[128, 16], [0, 2]], base=0,
                        channel_multiplier=1))
    PL(lambda e: e.iota(out=bigr[:], pattern=[[0, 3072]], base=BIG,
                        channel_multiplier=0))
    for cc in range(8):
        tl = dma("sync", "QS", stg128[0:8, 0:128],
                 BIAS6[:, 128 * cc:128 * (cc + 1)], wait=["DV", "PE"])
        tp = PE(lambda e: e.transpose(out=PS[:, 0:8], in_=stg128[0:8, 0:128],
                                      identity=c_idf[0:8, 0:8]), wait=[tl, "DV", ti])
        tbv = V(lambda e, c=cc: e.tensor_copy(out=battn[:, 8 * c:8 * c + 8],
                                              in_=PS[:, 0:8]), wait=[tp])
    V(lambda e: e.tensor_copy(
        out=lnbb[:], in_=battn[:].rearrange("p (c k) -> p c k", k=8)[:, :, 6]),
      wait=[tbv], flush=True)
    V(lambda e: e.tensor_copy(
        out=lngb[:], in_=battn[:].rearrange("p (c k) -> p c k", k=8)[:, :, 5]))
    for cc in range(8):
        tl = dma("sync", "QS", stg128[:, 0:8], WR[128 * cc:128 * (cc + 1), :],
                 wait=["DV"])
        V(lambda e, c=cc: e.tensor_copy(out=wrb[:, 8 * c:8 * c + 8],
                                        in_=stg128[:, 0:8]), wait=[tl])
    tok_prefill = None
    for i in range(NEXP):
        tok_prefill = dma("gpsimd", "QG",
                          d_tok[i].rearrange("s o -> (s o)")[None, :],
                          bigr[:, 0:768], wait=["PL"])
    t_brb = dma("sync", "QS", brb[:], BR[None, :].to_broadcast([128, 8]),
                wait=["DV"])

    xn_cm_v = AR[:, 0:32768].rearrange("p (c w) -> p c w", w=4096)
    NT_ALL = T_ALL // 128
    NT_BATCH = T_BATCH // 128
    NT_OWN = T_OWN // 128

    scr_cm_v = AR[:, 118 * 512:118 * 512 + 1024].rearrange("p (c w) -> p c w", w=128)
    wbuf = [arb(76, 8192), arb(92, 8192)]
    wstage = [arf(108, 1024), arf(112, 1024)]
    wq_list = [WQ1, WQ2, WK1, WK2, WV]

    wstg4 = [wstage[0], wstage[1], arf(146, 1024), arf(150, 1024)]
    QWPs = ["QWa", "QWb", "QWe", "QWf"]
    aw_cast_war = [None] * 4

    def load_attn_w(wi, into):
        casts = []
        for cc2 in range(8):
            w = [("PE", S.counts["PE"]), ("DV", S.counts["DV"]),
                 ("SC", S.counts["SC"])] if cc2 == 0 else []
            if aw_cast_war[cc2 % 4] is not None:
                w.append(aw_cast_war[cc2 % 4])
            tl = dma("gpsimd", QWPs[cc2 % 4], wstg4[cc2 % 4],
                     wq_list[wi][128 * cc2:128 * (cc2 + 1), :], wait=w)
            tc = V(lambda e, c=cc2, b=into: e.tensor_copy(
                out=wbuf[b][:, 1024 * c:1024 * (c + 1)], in_=wstg4[c % 4]),
                wait=[tl])
            aw_cast_war[cc2 % 4] = tc
            casts.append(tc)
        return casts

    if no_prefetch:
        wready = [[None] * 8]
    else:
        wready = [load_attn_w(0, 0)]

    xt_war = [None] * 4
    xn_war = [None] * 4
    pt_war = [None, None]
    st_war = [None] * 4
    mv_war = [None] * 4
    ps_war = None
    QXs = ["QX0", "QX1", "QX2", "QX3"]
    QNs = ["QN0", "QN1", "QN2", "QN3"]
    for t in range(NT_ALL):
        buf = t % 4
        b2 = t % 2
        xtile = arf(122 + 4 * buf, 1024)
        xntm = arb(138 + 2 * buf, 1024)
        bnb = bn12[:, 12 * buf:12 * buf + 12]
        mv = mvst[:, 8 * buf:8 * buf + 8]
        tl = dma("sync", QXs[buf], xtile,
                 X[128 * t:128 * (t + 1), :],
                 wait=(xt_war[buf] or [tok_prefill, ("QG", S.counts["QG"])]))
        V(lambda e, x=xtile, o=bnb: e.bn_stats(out=o[:, 0:6], in_=x[:, 0:512]),
          wait=[tl])
        tb0 = V(lambda e, x=xtile, o=bnb: e.bn_stats(out=o[:, 6:12],
                                                     in_=x[:, 512:1024]))
        tb = V(lambda e, o=bnb, m=mv: e.bn_aggr(out=m[:, 0:2], in_=o[:]),
               wait=[tb0, mv_war[buf]])
        tv = V(lambda e, m=mv: e.tensor_scalar(out=m[:, 4:5], in0=m[:, 1:2],
                                               scalar1=1e-5, scalar2=None,
                                               op0=ALU.add), wait=[tb])
        t1 = SCE(lambda e, m=mv: e.activation(out=m[:, 5:6], in_=m[:, 4:5],
                                              func=AF.Ln), wait=[tv])
        t2 = SCE(lambda e, m=mv: e.activation(out=m[:, 2:3], in_=m[:, 5:6],
                                              func=AF.Exp, scale=-0.5),
                 wait=[t1])
        t3 = V(lambda e, m=mv: e.tensor_tensor(out=m[:, 6:7], in0=m[:, 0:1],
                                               in1=m[:, 2:3], op=ALU.mult),
               wait=[t2, tb])
        t4 = V(lambda e, m=mv: e.tensor_scalar(out=m[:, 3:4], in0=m[:, 6:7],
                                               scalar1=-1.0, scalar2=None,
                                               op0=ALU.mult), wait=[t3])
        t5 = SCE(lambda e, x=xtile, o=xntm, m=mv: e.activation(
            out=o, in_=x, func=AF.Identity, bias=m[:, 3:4], scale=m[:, 2:3]),
            wait=[t4, t2] + (xn_war[buf] or []))
        town = None
        if t < NT_OWN:
            town = dma("sync", QNs[buf],
                       d_xn[128 * t:128 * (t + 1), :], xntm, wait=[t5])
        tps = None
        for cc in range(8):
            w = [t5]
            if cc == 0 and pt_war[b2] is not None:
                w.append(pt_war[b2])
            tps = PE(lambda e, c=cc, o=(PT0 if b2 == 0 else PT1), i=xntm:
                     e.transpose(out=o[:, 128 * c:128 * (c + 1)],
                                 in_=i[:, 128 * c:128 * (c + 1)],
                                 identity=c_idn[:]), wait=w)
        if t < NT_BATCH:
            cm_dst = xn_cm_v[:, :, 128 * t:128 * (t + 1)]
            cm_get = lambda c, tt=t: xn_cm_v[:, c, 128 * tt:128 * (tt + 1)]
        else:
            cm_dst = scr_cm_v
            cm_get = lambda c: scr_cm_v[:, c, :]
        tev = V(lambda e, o=cm_dst, p=(PT0 if b2 == 0 else PT1):
                e.tensor_copy(out=o,
                              in_=p[:].rearrange("p (c w) -> p c w", w=128)),
                wait=[tps])
        pt_war[b2] = tev
        mv_war[buf] = t5
        xt_war[buf] = [tb, t5]
        xn_war[buf] = [tps, town]
        tmm = None
        for cc in range(8):
            w = [tev]
            if cc == 0 and ps_war is not None:
                w.append(ps_war)
            tmm = PE(lambda e, c=cc, g=cm_get: e.matmul(
                out=PS[:, 0:8], lhsT=g(c),
                rhs=wrb[:, 8 * c:8 * c + 8], start=(c == 0), stop=(c == 7)),
                wait=w)
        st8 = stile[:, 8 * buf:8 * buf + 8]
        w = [tmm, t_brb]
        if st_war[buf] is not None:
            w.append(st_war[buf])
        lg8 = sc16[:, 8 * buf:8 * buf + 8]
        tadd = V(lambda e, o=lg8: e.tensor_tensor(
            out=o, in0=PS[:, 0:8], in1=brb[:], op=ALU.add), wait=w)
        ps_war = tadd
        te = SCE(lambda e, o=st8, i=lg8, b=buf: e.activation(
            out=o, in_=i, func=AF.Exp, accum_out=l8[:, b:b + 1]),
            wait=[tadd])
        trp = V(lambda e, b=buf: e.reciprocal(out=l8[:, 4 + b:5 + b],
                                              in_=l8[:, b:b + 1]), wait=[te])
        tsc = V(lambda e, o=scal[:].rearrange("p (k w) -> p k w", w=128)[:, :, t:t + 1],
                i=st8, b=buf: e.tensor_scalar(
                    out=o, in0=i.rearrange("p (k w) -> p k w", w=1),
                    scalar1=l8[:, 4 + b:5 + b], scalar2=None, op0=ALU.mult),
                wait=[trp])
        st_war[buf] = tsc

    tap("scores", scal[:])
    tap("mv", mvst[:])
    tap("stile", stile[:])
    tap("l8", l8[:])
    tap("xn0_bf", xn_cm_v[:, 0, 0:1024])
    if last_phase <= 1:
        return finish_early()

    bis_cmp = arf(118, 1024)
    cntb = stile[:].bitcast(BF16)
    V(lambda e: e.memset(panel[:, 0:8], 0.0),
      wait=[tok_prefill, ("QO", S.counts["QO"])])
    V(lambda e: e.memset(stile[:, 0:16], 0.0))
    V(lambda e: e.memset(panel[:, 8:16], 1.0))
    bis_pe_war = [None]

    def bisect_iter(it):
        V(lambda e: e.tensor_tensor(out=panel[:, 24:32], in0=panel[:, 0:8],
                                    in1=panel[:, 8:16], op=ALU.add), flush=True)
        tm = V(lambda e: e.tensor_scalar(out=panel[:, 16:24], in0=panel[:, 24:32],
                                         scalar1=0.5, scalar2=None, op0=ALU.mult),
               flush=True)
        tc = V(lambda e: e.tensor_tensor(
            out=bis_cmp.rearrange("p (k w) -> p k w", w=128),
            in0=scal[:].rearrange("p (k w) -> p k w", w=128),
            in1=panel[:, 16:24].rearrange("p (k o) -> p k o", o=1).to_broadcast(
                [128, 8, 128]),
            op=ALU.is_gt), wait=[tm], flush=True)
        tn = V(lambda e: e.tensor_reduce(
            out=bis[:, 0:8], in_=bis_cmp.rearrange("p (k w) -> p k w", w=128),
            axis=AX.X, op=ALU.add), wait=[tc], flush=True)
        tnb = V(lambda e: e.tensor_copy(out=cntb[:, 0:8], in_=bis[:, 0:8]),
                wait=[tn], flush=True)
        wpe = [tnb]
        if bis_pe_war[0] is not None:
            wpe.append(bis_pe_war[0])
        tp = PE(lambda e: e.matmul(out=PS[:, 24:32], lhsT=c_os[:],
                                   rhs=cntb[:, 0:8], start=True, stop=True),
                wait=wpe)
        tg = V(lambda e: e.tensor_scalar(out=panel[:, 24:32], in0=PS[:, 24:32],
                                         scalar1=KTOP, scalar2=None,
                                         op0=ALU.is_ge), wait=[tp])
        bis_pe_war[0] = tg
        V(lambda e: e.tensor_tensor(out=stile[:, 0:8], in0=panel[:, 16:24],
                                    in1=panel[:, 0:8], op=ALU.subtract),
          wait=[tg], flush=True)
        V(lambda e: e.tensor_tensor(out=stile[:, 0:8], in0=stile[:, 0:8],
                                    in1=panel[:, 24:32], op=ALU.mult), flush=True)
        V(lambda e: e.tensor_tensor(out=panel[:, 0:8], in0=panel[:, 0:8],
                                    in1=stile[:, 0:8], op=ALU.add), flush=True)
        V(lambda e: e.tensor_tensor(out=stile[:, 8:16], in0=panel[:, 8:16],
                                    in1=panel[:, 16:24], op=ALU.subtract),
          flush=True)
        V(lambda e: e.tensor_tensor(out=stile[:, 8:16], in0=stile[:, 8:16],
                                    in1=panel[:, 24:32], op=ALU.mult), flush=True)
        V(lambda e: e.tensor_tensor(out=panel[:, 8:16], in0=panel[:, 16:24],
                                    in1=stile[:, 8:16], op=ALU.add),
          flush=True)

    pdst = [d_q1, d_q2, d_k1, d_k2, d_v]
    pstage = [arb(116, 512), arb(117, 512)]
    pst_war = [None, None]
    pps_war = [None, None]

    def make_proj_unit(wi, oc, tg, pbuf):
        def emit_unit():
            tmm = None
            for cc in range(8):
                w = [wready[wi][cc]]
                if cc == 0 and pps_war[pbuf] is not None:
                    w.append(pps_war[pbuf])
                tmm = PE(lambda e, c=cc: e.matmul(
                    out=(PA0 if pbuf == 0 else PA1)[:],
                    lhsT=wbuf[wi % 2][:, 1024 * c + 128 * oc:1024 * c + 128 * oc + 128],
                    rhs=xn_cm_v[:, c, 512 * tg:512 * (tg + 1)],
                    start=(c == 0), stop=(c == 7)), wait=w)
            tst = V(lambda e: e.scalar_tensor_tensor(
                out=pstage[pbuf], in0=(PA0 if pbuf == 0 else PA1)[:],
                scalar=battn[:, 8 * oc + wi:8 * oc + wi + 1],
                in1=xn_cm_v[:, oc, 512 * tg:512 * (tg + 1)],
                op0=ALU.add, op1=ALU.add), wait=[tmm] + (pst_war[pbuf] or []))
            pps_war[pbuf] = tst
            td = dma("sync", "QP0" if pbuf == 0 else "QP1",
                     pdst[wi][128 * oc:128 * (oc + 1), 512 * tg:512 * (tg + 1)],
                     pstage[pbuf], wait=[tst])
            pst_war[pbuf] = [td]
        return emit_unit

    proj_units = []
    seq = 0
    for wi in range(5):
        ng = 4 if wi < 2 else 8
        for oc in range(8):
            for tg in range(ng):
                proj_units.append((wi, make_proj_unit(wi, oc, tg, seq % 2)))
                seq += 1

    loads_started = {0: True}
    ui = 0
    per = max(1, len(proj_units) // NBIS)
    for it in range(NBIS):
        bisect_iter(it)
        budget = per if it < NBIS - 1 else len(proj_units) - ui
        for _ in range(budget):
            if ui >= len(proj_units):
                break
            wi, eu = proj_units[ui]
            eu()
            ui += 1
            done_of_wi = sum(1 for k in range(ui) if proj_units[k][0] == wi)
            total_of_wi = sum(1 for u in proj_units if u[0] == wi)
            if (wi + 1) < 5 and (wi + 1) not in loads_started and \
                    done_of_wi >= (3 * total_of_wi) // 4:
                wready.append(load_attn_w(wi + 1, (wi + 1) % 2))
                loads_started[wi + 1] = True
    while ui < len(proj_units):
        wi, eu = proj_units[ui]
        if (wi + 1) < 5 and (wi + 1) not in loads_started:
            wready.append(load_attn_w(wi + 1, (wi + 1) % 2))
            loads_started[wi + 1] = True
        eu()
        ui += 1

    ttau = V(lambda e: e.tensor_copy(out=bis[:, 8:16], in_=panel[:, 0:8]),
             flush=True)
    tap("tau", bis[:, 0:16])
    if last_phase <= 2:
        return finish_early()

    slot_tok = []
    for e_i in range(NEXP):
        V(lambda e, k=e_i: e.tensor_tensor(
            out=mask8[:, 16 * k:16 * (k + 1)], in0=scal[:, 128 * k:128 * k + 16],
            in1=bis[:, 8 + k:9 + k].to_broadcast([128, 16]), op=ALU.is_gt),
            wait=[ttau])
        V(lambda e, k=e_i: e.tensor_copy(out=sc16[:, 0:16],
                                         in_=mask8[:, 16 * k:16 * (k + 1)]),
          flush=True)
        cur = 0
        for sh in (1, 2, 4, 8):
            V(lambda e, c=cur, s=sh: e.tensor_copy(
                out=sc16[:, 16 * (1 - c):16 * (1 - c) + s],
                in_=sc16[:, 16 * c:16 * c + s]), flush=True)
            V(lambda e, c=cur, s=sh: e.tensor_tensor(
                out=sc16[:, 16 * (1 - c) + s:16 * (1 - c) + 16],
                in0=sc16[:, 16 * c + s:16 * c + 16],
                in1=sc16[:, 16 * c:16 * c + 16 - s], op=ALU.add),
                flush=True)
            cur = 1 - cur
        tbt = V(lambda e, c=cur: e.tensor_copy(out=tmpb[:, 0:1],
                                               in_=sc16[:, 16 * c + 15:16 * c + 16]),
                flush=True)
        tpx = PE(lambda e: e.matmul(out=PS[:, 8:9], lhsT=c_slt[:], rhs=tmpb[:, 0:1],
                                    start=True, stop=True), wait=[tbt, "SC", "DV"])
        tevp = V(lambda e: e.tensor_copy(out=sc16[:, 32:33], in_=PS[:, 8:9]),
                 wait=[tpx])
        tsl = V(lambda e, c=cur: e.tensor_scalar(
            out=sc16[:, 16 * (1 - c):16 * (1 - c) + 16],
            in0=sc16[:, 16 * c:16 * c + 16], scalar1=sc16[:, 32:33], scalar2=-1.0,
            op0=ALU.add, op1=ALU.add), wait=[tevp], flush=True)
        cur = 1 - cur
        V(lambda e, c=cur: e.tensor_scalar(
            out=sc16[:, 16 * c:16 * c + 16], in0=sc16[:, 16 * c:16 * c + 16],
            scalar1=-float(BIG), scalar2=None, op0=ALU.add), wait=[tsl])
        V(lambda e, k=e_i, c=cur: e.tensor_tensor(
            out=slotf[:], in0=sc16[:, 16 * c:16 * c + 16],
            in1=mask8[:, 16 * k:16 * (k + 1)], op=ALU.mult),
            wait=[("PL", S.counts["PL"])], flush=True)
        tse = V(lambda e: e.tensor_scalar(
            out=slotf[:], in0=slotf[:], scalar1=float(BIG), scalar2=None,
            op0=ALU.add), flush=True)
        tcv = PL(lambda e, k=e_i: e.tensor_copy(out=slot8[:, 16 * k:16 * (k + 1)],
                                                in_=slotf[:]), wait=[tse], flush=True)
        slot_tok.append(tcv)
        V(lambda e, k=e_i: e.tensor_tensor(
            out=gmask8[:, 16 * k:16 * (k + 1)], in0=mask8[:, 16 * k:16 * (k + 1)],
            in1=scal[:, 128 * k:128 * k + 16], op=ALU.mult))
    tap("mask", mask8[:])
    if last_phase <= 2.7:
        return finish_early()

    W2bf = arb(0, 32768)
    W1q = [arb(64, 8192), arb(80, 8192)]
    h1 = arb(96, 12288)
    wst3 = [arf(120, 2048), arf(128, 2048)]
    xcomp = arb(136, 3072)
    xcm = arb(142, 3072)
    b2b = arf(148, 1024)
    ctrb = [arf(152, 512), arf(154, 512)]

    V(lambda e: e.memset(AR[:, 136 * 512:148 * 512].bitcast(F32), 0.0),
      wait=["PE", "QS", "SC", tok_prefill])
    xc_zero = S.counts["DV"]

    scat_done = {}

    def emit_scatters(e_i):
        last = None
        for j in range(16):
            w = [slot_tok[e_i], tok_prefill]
            last = idma_s(d_tok[e_i],
                          iotac[:, 2 * j:2 * j + 2],
                          slot8[:, 16 * e_i + j:16 * e_i + j + 1], CAP - 1, wait=w)
        scat_done[e_i] = last

    emit_scatters(0)
    if last_phase <= 2.8:
        return finish_early()

    wstg1 = [wst3[0][:, 0:1024], wst3[1][:, 0:1024],
             arf(156, 1024), arf(160, 1024)]
    wstg2 = [wst3[0][:, 1024:2048], wst3[1][:, 1024:2048],
             arf(164, 1024), arf(168, 1024)]
    QW1s = ["QWa", "QWb", "QWc", "QWd"]
    QW2s = ["QW2a", "QW2b", "QW2c", "QW2d"]
    w1_cast_war = [None] * 4
    w2_cast_war = [None] * 4

    def load_w1q(e_i, q, into):
        toks_ = []
        for cc in range(8):
            w = [("PE", S.counts["PE"]), ("DV", S.counts["DV"]),
                 ("SC", S.counts["SC"])] if cc == 0 else []
            if w1_cast_war[cc % 4] is not None:
                w.append(w1_cast_war[cc % 4])
            tl = dma("gpsimd", QW1s[cc % 4], wstg1[cc % 4],
                     FW1[e_i, 128 * cc:128 * (cc + 1), 1024 * q:1024 * (q + 1)],
                     wait=w)
            tc = V(lambda e, c=cc, b=into: e.tensor_copy(
                out=W1q[b][:, 1024 * c:1024 * (c + 1)], in_=wstg1[c % 4]),
                wait=[tl])
            w1_cast_war[cc % 4] = tc
            toks_.append(tc)
        return toks_

    def load_w2(e_i):
        toks_ = []
        for f2 in range(32):
            w = [("PE", S.counts["PE"])] if f2 == 0 else []
            if w2_cast_war[f2 % 4] is not None:
                w.append(w2_cast_war[f2 % 4])
            tl = dma("sync", QW2s[f2 % 4], wstg2[f2 % 4],
                     FW2[e_i, 128 * f2:128 * (f2 + 1), :], wait=w)
            tc = V(lambda e, f=f2: e.tensor_copy(
                out=W2bf[:, 1024 * f:1024 * (f + 1)], in_=wstg2[f % 4]),
                wait=[tl])
            w2_cast_war[f2 % 4] = tc
            toks_.append(tc)
        return toks_

    w1_cast_war[0] = tok_prefill
    w1_cast_war[1] = tok_prefill
    w1ready = {(0, 0): load_w1q(0, 0, 0)}
    xcq_war = None
    pt_ev = []
    ctr_store_hist = []
    relu_war = [None, None]
    mm2_war = [None, None]

    for e_i in range(NEXP):
        if e_i + 1 < NEXP:
            emit_scatters(e_i + 1)
        tkl = dma("gpsimd", "QG", toks[:],
                  d_tok[e_i].rearrange("(c p) o -> p c o", p=128)[:, :, 0],
                  wait=[scat_done[e_i]], allow_slow_non_contiguous=True)
        if last_phase <= 2.82:
            return finish_early()
        xg_toks = []
        for sc_i in range(3):
            w = [tkl]
            if e_i == 0 and sc_i == 0:
                w += [("DV", xc_zero), ("QS", S.counts["QS"]),
                      ("QO", S.counts["QO"])]
            if xcq_war is not None:
                w.append(xcq_war)
            xg_toks.append(idma_g(
                xcomp[:, 1024 * sc_i:1024 * (sc_i + 1)], d_xn,
                toks[:, sc_i:sc_i + 1], T_OWN - 1, wait=w))
        if last_phase <= 2.84:
            return finish_early()
        tevx = None
        for kc in range(8):
            txc = None
            for sc_i in range(3):
                w = [xg_toks[sc_i]]
                if sc_i == 0 and len(pt_ev) >= 2:
                    w.append(pt_ev[-2])
                txc = PE(lambda e, k=kc, s=sc_i,
                         pt=(PT0 if kc % 2 == 0 else PT1):
                         e.transpose(
                             out=pt[:, 384 * ((k // 2) % 2) + 128 * s:
                                    384 * ((k // 2) % 2) + 128 * (s + 1)],
                             in_=xcomp[:, 1024 * s + 128 * k:1024 * s + 128 * (k + 1)],
                             identity=c_idn[:]), wait=w)
            tevx = V(lambda e, k=kc, pt=(PT0 if kc % 2 == 0 else PT1):
                     e.tensor_copy(
                         out=xcm[:, 384 * k:384 * (k + 1)],
                         in_=pt[:, 384 * ((k // 2) % 2):384 * ((k // 2) % 2) + 384]),
                     wait=[txc])
            pt_ev.append(tevx)
            if last_phase <= 2.84 + 0.0001 * (kc + 1):
                return finish_early()
        xcq_war = tevx
        if last_phase <= 2.86:
            return finish_early()
        tb1l = dma("sync", "QS", stg128[0:32, 0:128],
                   FB1[e_i].rearrange("(a b) -> a b", b=128),
                   wait=[("PE", S.counts["PE"])])
        tb1t = PE(lambda e: e.transpose(out=PS[:, 32:64], in_=stg128[0:32, 0:128],
                                        identity=c_idf[0:32, 0:32]),
                  wait=[tb1l, ("DV", S.counts["DV"])])
        tb1e = V(lambda e: e.tensor_copy(out=b1c[:], in_=PS[:, 32:64]),
                 wait=[tb1t, ("SC", S.counts["SC"])])
        tb2 = dma("sync", "QS", b2b, FB2[e_i][None, :].to_broadcast([128, 1024]),
                  wait=[("DV", S.counts["DV"])])
        w2toks = load_w2(e_i)
        relu_toks = []
        for q in range(4):
            wq = w1ready.pop((e_i, q))
            if q < 3:
                w1ready[(e_i, q + 1)] = load_w1q(e_i, q + 1, (q + 1) % 2)
            elif e_i < NEXP - 1:
                w1ready[(e_i + 1, 0)] = load_w1q(e_i + 1, 0, 0)
            for f_i in range(8):
                fg = 8 * q + f_i
                pb = fg % 2
                tmm = None
                for cc in range(8):
                    w = [wq[cc], xcq_war]
                    if cc == 0 and relu_war[pb] is not None:
                        w.append(relu_war[pb])
                    tmm = PE(lambda e, c=cc, f=f_i, p=pb, qq=q: e.matmul(
                        out=(PA0 if p == 0 else PA1)[:, 0:384],
                        lhsT=W1q[qq % 2][:, 1024 * c + 128 * f:1024 * c + 128 * (f + 1)],
                        rhs=xcm[:, 384 * c:384 * (c + 1)],
                        start=(c == 0), stop=(c == 7)), wait=w)
                wr_ = [tmm, tb1e]
                if e_i == 0 and fg == 0:
                    wr_ += [("QP0", S.counts["QP0"]), ("QP1", S.counts["QP1"])]
                tr = SCE(lambda e, g=fg, p=pb: e.activation(
                    out=h1[:, 384 * g:384 * (g + 1)],
                    in_=(PA0 if p == 0 else PA1)[:, 0:384],
                    func=AF.Relu, bias=b1c[:, g:g + 1], scale=1.0),
                    wait=wr_, flush=True)
                relu_war[pb] = tr
                relu_toks.append(tr)
        if last_phase <= 2.88:
            return finish_early()
        for sc_i in range(3):
            for j in range(2):
                pb = (sc_i * 2 + j) % 2
                tmm = None
                for f in range(32):
                    w = [relu_toks[f], w2toks[f]]
                    if f == 0 and mm2_war[pb] is not None:
                        w.append(mm2_war[pb])
                    tmm = PE(lambda e, ff=f, s=sc_i, jj=j, p=pb: e.matmul(
                        out=(PB0 if p == 0 else PB1)[:],
                        lhsT=h1[:, 384 * ff + 128 * s:384 * ff + 128 * (s + 1)],
                        rhs=W2bf[:, 1024 * ff + 512 * jj:1024 * ff + 512 * (jj + 1)],
                        start=(ff == 0), stop=(ff == 31)), wait=w)
                wta = [tmm, tb2]
                if len(ctr_store_hist) >= 2:
                    wta.append(ctr_store_hist[-2])
                ta = V(lambda e, p=pb, jj=j: e.tensor_tensor(
                    out=ctrb[p], in0=(PB0 if p == 0 else PB1)[:],
                    in1=b2b[:, 512 * jj:512 * (jj + 1)], op=ALU.add), wait=wta,
                    flush=True)
                mm2_war[pb] = ta
                tstr = dma("sync", "QC0" if pb == 0 else "QC1",
                           d_ctr[e_i][128 * sc_i:128 * (sc_i + 1),
                                      512 * j:512 * (j + 1)], ctrb[pb], wait=[ta])
                ctr_store_hist.append(tstr)

    tap("gates", gmask8[:, 0:128])
    if last_phase <= 3:
        return finish_early()

    yb = [arf(64, 1024), arf(68, 1024)]
    gacc = [arf(72, 1024), arf(76, 1024), arf(84, 1024), arf(88, 1024)]
    gt4 = [arb(80, 1024), arb(82, 1024)]
    tz0 = V(lambda e: e.memset(gacc[0], 0.0), wait=[("PE", S.counts["PE"])])
    gacc_war = [tz0]
    for _z in range(1, 4):
        gacc_war.append(V(lambda e, z=_z: e.memset(gacc[z], 0.0)))
    gt_war = [None, None]
    silu_prev = None
    ctr_all = [("QC0", S.counts["QC0"]), ("QC1", S.counts["QC1"])]
    ytok = None
    QGAs = ["QGA0", "QGA1", "QGA2", "QGA3"]
    for t in range(NT_OWN):
        for e_i in range(NEXP):
            gb = e_i % 4
            tg = idma_g2(gacc[gb], d_ctr[e_i],
                         slot8[:, 16 * e_i + t:16 * e_i + t + 1],
                         CAP - 1, QGAs[gb],
                         wait=ctr_all + [gacc_war[gb]])
            if e_i == 0:
                ytok = V(lambda e, k=e_i, tt=t: e.tensor_scalar(
                    out=yb[0], in0=gacc[0],
                    scalar1=gmask8[:, 16 * k + tt:16 * k + tt + 1],
                    scalar2=None, op0=ALU.mult), wait=[tg, ytok])
            else:
                wy = [tg, ytok]
                if e_i == 1 and silu_prev is not None:
                    wy.append(silu_prev)
                ytok = V(lambda e, k=e_i, tt=t, g=gb, a=e_i % 2:
                         e.scalar_tensor_tensor(
                    out=yb[a], in0=gacc[g],
                    scalar=gmask8[:, 16 * k + tt:16 * k + tt + 1],
                    in1=yb[1 - a], op0=ALU.mult, op1=ALU.add), wait=wy)
            gacc_war[gb] = ytok
        w = [ytok]
        if gt_war[t % 2] is not None:
            w.append(gt_war[t % 2])
        tsl = SCE(lambda e, tt=t: e.activation(out=gt4[tt % 2], in_=yb[1],
                                               func=AF.Silu), wait=w, flush=True)
        silu_prev = tsl
        gt_war[t % 2] = dma("sync", "QD0" if t % 2 == 0 else "QD1",
                            d_g[128 * t:128 * (t + 1), :], gt4[t % 2], wait=[tsl])

    if last_phase <= 4:
        return finish_early()

    OC_v = AR[:, 0:16384].rearrange("p (c w) -> p c w", w=2048)
    hb = [
        dict(q1=AR[0:64, 16384:18432], q2=AR[0:64, 18432:20480],
             k1=AR[0:64, 20480:24576], k2=AR[0:64, 24576:28672],
             vc=AR[0:64, 28672:32768]),
        dict(q1=AR[0:64, 32768:34816], q2=AR[0:64, 34816:36864],
             k1=AR[0:64, 36864:40960], k2=AR[0:64, 40960:45056],
             vc=AR[0:64, 45056:49152]),
    ]
    vtm = [arb(96, 2048), arb(100, 2048)]
    spn = [arf(104, 4096), arf(120, 4096)]
    ppn = [arb(136 + 8 * i, 4096) for i in range(4)]
    pTb = [arb(168, 1024), arb(170, 1024)]
    sS1 = arf(172, 1024)

    head_use = [None, None]
    vtm_use = [None, None]

    def head_load(h):
        b = h % 2
        q = "QH0" if b == 0 else "QH1"
        w = [head_use[b]] if head_use[b] is not None else [
            ("PE", S.counts["PE"]), ("DV", S.counts["DV"]),
            ("SC", S.counts["SC"]), ("QD0", S.counts["QD0"]),
            ("QD1", S.counts["QD1"]), ("QGA0", S.counts["QGA0"]),
            ("QGA1", S.counts["QGA1"]), ("QGA2", S.counts["QGA2"]),
            ("QGA3", S.counts["QGA3"])]
        dma("sync", q, hb[b]["q1"], d_q1[64 * h:64 * (h + 1), :], wait=w)
        dma("sync", q, hb[b]["q2"], d_q2[64 * h:64 * (h + 1), :])
        dma("sync", q, hb[b]["k1"], d_k1[64 * h:64 * (h + 1), :])
        dma("sync", q, hb[b]["k2"], d_k2[64 * h:64 * (h + 1), :])
        return dma("sync", q, hb[b]["vc"], d_v[64 * h:64 * (h + 1), :])

    psb_ev = [None]

    def head_vtm(h, wait_tok):
        b = h % 2
        for g4 in range(4):
            tt = None
            for i in range(8):
                kc = 8 * g4 + i
                w = [wait_tok]
                if i == 0 and psb_ev[0] is not None:
                    w.append(psb_ev[0])
                tt = PE(lambda e, k=kc, ii=i, b=b: e.transpose(
                    out=PSB[:, 64 * ii:64 * ii + 64],
                    in_=hb[b]["vc"][:, 128 * k:128 * (k + 1)],
                    identity=c_idn[0:64, 0:64]), wait=w)
            wev = [tt]
            if g4 == 0 and vtm_use[b] is not None:
                wev.append(vtm_use[b])
            psb_ev[0] = V(lambda e, g=g4, b=b: e.tensor_copy(
                out=vtm[b][:, 512 * g:512 * (g + 1)], in_=PSB[:, 0:512]),
                wait=wev)
        return psb_ev[0]

    tload = head_load(0)
    tvtm = head_vtm(0, tload)
    head_ready = {0: (tload, tvtm)}
    sp_war = [None, None]
    ppn_war = [None, None]
    pt_ev5 = [None, None]
    sS1_war = [None, None]
    psum_war = [None, None]
    oc_done = []
    nl = None
    nv = None
    hload = hvtm = None
    last_tm2 = None
    pending = None
    carry_exp = []

    def emit_ladder(h, u, sp, hload, sce_fill=None, pe_fill=None):
        pset = (u % 2) * 2
        qt = 2 * u + sp
        mult_toks = []
        for kb in range(8):
            pb = kb % 2
            w1 = [hload]
            if kb < 2:
                if psum_war[pb] is not None:
                    w1.append(psum_war[pb])
            else:
                w1.append(mult_toks[kb - 2])
            tm1 = PE(lambda e, k=kb, bb=h % 2, q=qt, p=pb: e.matmul(
                out=(PA0 if p == 0 else PA1)[:],
                lhsT=hb[bb]["q1"][:, 128 * q:128 * (q + 1)],
                rhs=hb[bb]["k1"][:, 512 * k:512 * (k + 1)],
                start=True, stop=True), wait=w1)
            tm2 = PE(lambda e, k=kb, bb=h % 2, q=qt, p=pb: e.matmul(
                out=(PB0 if p == 0 else PB1)[:],
                lhsT=hb[bb]["q2"][:, 128 * q:128 * (q + 1)],
                rhs=hb[bb]["k2"][:, 512 * k:512 * (k + 1)],
                start=True, stop=True))
            if pe_fill is not None and kb >= 2:
                next(pe_fill, None)
            wsc = [tm1]
            if sS1_war[pb] is not None:
                wsc.append(sS1_war[pb])
            tevac = SCE(lambda e, p=pb: e.copy(
                out=sS1[:, 512 * p:512 * (p + 1)],
                in_=(PA0 if p == 0 else PA1)[:]), wait=wsc)
            if sce_fill:
                sce_fill.pop(0)()
            wmu = [tevac, tm2]
            if kb == 0 and sp_war[sp] is not None:
                wmu.append(sp_war[sp])
            tmul = V(lambda e, k=kb, p=pb, s=sp: e.tensor_tensor(
                out=spn[s][:, 512 * k:512 * (k + 1)],
                in0=sS1[:, 512 * p:512 * (p + 1)],
                in1=(PB0 if p == 0 else PB1)[:], op=ALU.mult), wait=wmu)
            mult_toks.append(tmul)
            sS1_war[pb] = tmul
            psum_war[pb] = tmul
            if kb >= 1:
                V(lambda e, k=kb - 1, s=sp: e.tensor_reduce(
                    out=panel[:, 8 * s + k:8 * s + k + 1],
                    in_=spn[s][:, 512 * k:512 * (k + 1)], axis=AX.X,
                    op=ALU.max), wait=[mult_toks[kb - 1]])
        V(lambda e, s=sp: e.tensor_reduce(
            out=panel[:, 8 * s + 7:8 * s + 8],
            in_=spn[s][:, 512 * 7:512 * 8], axis=AX.X,
            op=ALU.max), wait=[mult_toks[7]], flush=True)
        tred = V(lambda e, s=sp: e.tensor_reduce(
            out=panel[:, 16 + s:17 + s], in_=panel[:, 8 * s:8 * s + 8],
            axis=AX.X, op=ALU.max, negate=True),
            wait=[sp_war[sp]], flush=True)
        return tred, tm2

    def make_exp_chunks(h, u, sp, tred, cell):
        pset = (u % 2) * 2
        chunks = []
        for c in range(4):
            def emit_c(c=c):
                wex = [tred]
                if c == 0 and ppn_war[u % 2] is not None:
                    wex.append(ppn_war[u % 2])
                tk = SCE(lambda e, s=sp, ps=pset, cc=c: e.activation(
                    out=ppn[ps + s][:, 1024 * cc:1024 * (cc + 1)],
                    in_=spn[s][:, 1024 * cc:1024 * (cc + 1)], func=AF.Exp,
                    bias=panel[:, 16 + s:17 + s]), wait=wex)
                sp_war[sp] = tk
                cell[c] = tk
            chunks.append(emit_c)
        return chunks

    def pv_group(st, j):
        h, u = st["h"], st["u"]
        r = None
        for z in range(4):
            kc = 4 * j + z
            w = [st["ev2"][j % 2], st["hvtm"]] if z == 0 else []
            if kc == 0 and oc_done:
                w.append(oc_done[-1])
            r = PE(lambda e, k=kc, bb=h % 2, jj=j: e.matmul(
                out=PS[0:64, 0:256],
                lhsT=vtm[bb][:, 64 * k:64 * (k + 1)],
                rhs=pTb[jj % 2][:, 256 * (k % 4):256 * (k % 4 + 1)],
                start=(k == 0), stop=(k == 31)), wait=w)
        return r

    def tpv_micro(st):
        h, u = st["h"], st["u"]
        pset = (u % 2) * 2
        texp2 = st["texp2"]
        ttg = None
        for j in range(8):
            for zh in range(2):
                for z in range(2 * zh, 2 * zh + 2):
                    kc = 4 * j + z
                    for sp in range(2):
                        w = []
                        if z == 0 and sp == 0:
                            w = [texp2[0][j // 2], texp2[1][j // 2]]
                            if pt_ev5[j % 2] is not None:
                                w.append(pt_ev5[j % 2])
                        ttg = PE(lambda e, k=kc, zz=z, s=sp, jj=j, ps=pset:
                                 e.transpose(
                                     out=(PT0 if jj % 2 == 0 else PT1)[
                                         :, 256 * zz + 128 * s:
                                         256 * zz + 128 * (s + 1)],
                                     in_=ppn[ps + s][:, 128 * k:128 * (k + 1)],
                                     identity=c_idn[:]), wait=w)
                yield
            if j >= 1:
                st["pv2"][(j - 1) % 2] = pv_group(st, j - 1)
                yield
            wev = [ttg]
            if st["pv2"][j % 2] is not None:
                wev.append(st["pv2"][j % 2])
            if j % 2 == 0:
                tevg = SCE(lambda e, jj=j: e.copy(
                    out=pTb[jj % 2][:], in_=(PT0 if jj % 2 == 0 else PT1)[:]),
                    wait=wev)
            else:
                tevg = V(lambda e, jj=j: e.tensor_copy(
                    out=pTb[jj % 2][:], in_=(PT0 if jj % 2 == 0 else PT1)[:]),
                    wait=wev)
            pt_ev5[j % 2] = tevg
            st["ev2"][j % 2] = tevg
        pv_last = pv_group(st, 7)
        ppn_war[u % 2] = ttg
        st["pv_last"] = pv_last
        cchunk = h // 2
        if h % 2 == 0:
            tocd = V(lambda e, cch=cchunk, uu=u: e.tensor_copy(
                out=OC_v[0:64, cch, 256 * uu:256 * (uu + 1)],
                in_=PS[0:64, 0:256]), wait=[pv_last])
        else:
            tsh1 = V(lambda e: e.tensor_copy(out=oshift[:], in_=PS[0:64, 0:256]),
                     wait=[pv_last, ("QO", S.counts["QO"])])
            tocd = dma("sync", "QO",
                       OC_v[64:128, cchunk, 256 * u:256 * (u + 1)],
                       oshift[:], wait=[tsh1])
        oc_done.append(tocd)
        yield

    for g in range(HEADS * 8):
        h, u = g // 8, g % 8
        if u == 0:
            hload, hvtm = head_ready.pop(h)
            nl = head_load(h + 1) if h + 1 < HEADS else None
            nv = None
        if u == 3 and nl is not None:
            nv = head_vtm(h + 1, nl)
        filler = tpv_micro(pending) if pending is not None else iter(())
        tredA, _ = emit_ladder(h, u, 0, hload, sce_fill=carry_exp,
                               pe_fill=filler)
        for fn in carry_exp:
            fn()
        cellA = [None] * 4
        chunksA = make_exp_chunks(h, u, 0, tredA, cellA)
        tredB, last_tm2 = emit_ladder(h, u, 1, hload, sce_fill=chunksA,
                                      pe_fill=filler)
        for fn in chunksA:
            fn()
        for _ in filler:
            pass
        if pending is not None and pending["u"] == 7:
            vtm_use[pending["h"] % 2] = pending["pv_last"]
        cellB = [None] * 4
        carry_exp = make_exp_chunks(h, u, 1, tredB, cellB)
        pending = {"h": h, "u": u, "texp2": [cellA, cellB], "hvtm": hvtm,
                   "ev2": [None, None], "pv2": [None, None], "pv_last": None}
        if last_phase <= 5 + 0.0001 * (g + 1):
            return finish_early()
        if u == 7:
            if nl is not None:
                head_ready[h + 1] = (nl, nv)
            head_use[h % 2] = last_tm2
    for fn in carry_exp:
        fn()
    for _ in tpv_micro(pending):
        pass
    pv_tok = pending["pv_last"]
    vtm_use[pending["h"] % 2] = pv_tok
    head_use[0] = pv_tok
    head_use[1] = pv_tok

    if last_phase <= 5:
        return finish_early()

    o2t = arb(152, 4096)
    wall = [("DV", S.counts["DV"]), ("QO", S.counts["QO"]),
            ("SC", S.counts["SC"]), ("PE", S.counts["PE"])]
    psrow = [PA0, PA1, PB0, PB1]
    tsum_mm = [None] * 4
    for cc in range(8):
        for g in range(4):
            tsum_mm[g] = PE(lambda e, c=cc, gg=g: e.matmul(
                out=psrow[gg][0:1, 0:512], lhsT=c_one[:],
                rhs=OC_v[:, c, 512 * gg:512 * (gg + 1)],
                start=(c == 0), stop=(c == 7)),
                wait=(wall if (cc == 0 and g == 0) else []))
    sum_ev = [V(lambda e, gg=g: e.tensor_copy(
        out=orow[:, 512 * gg:512 * (gg + 1)], in_=psrow[gg][0:1, 0:512]),
        wait=[tsum_mm[g]]) for g in range(4)]
    tr_ev = None
    for sl in range(16):
        w = [sum_ev[3]] if sl == 0 else [tr_ev]
        tt = PE(lambda e, s=sl: e.transpose(
            out=PS[:, 64 + (s % 2):65 + (s % 2)],
            in_=orow[:, 128 * s:128 * (s + 1)],
            identity=c_idf[0:1, 0:1]), wait=w)
        tr_ev = V(lambda e, s=sl: e.tensor_copy(
            out=ostat[:, s:s + 1], in_=PS[:, 64 + (s % 2):65 + (s % 2)]),
            wait=[tt])
    tsq_prev = [None, None]
    tsq_mm = [None] * 4
    for cc in range(8):
        w = list(wall) if tsq_prev[cc % 2] is None else [tsq_prev[cc % 2]]
        tsq = SCE(lambda e, c=cc: e.activation(
            out=o2t[:, 2048 * (c % 2):2048 * (c % 2) + 2048], in_=OC_v[:, c, :],
            func=AF.Square), wait=w, flush=True)
        for g in range(4):
            wmm = [tsq]
            if cc == 0:
                wmm.append(sum_ev[g])
            tsq_mm[g] = PE(lambda e, c=cc, gg=g: e.matmul(
                out=psrow[gg][0:1, 0:512], lhsT=c_one[:],
                rhs=o2t[:, 2048 * (c % 2) + 512 * gg:
                        2048 * (c % 2) + 512 * (gg + 1)],
                start=(c == 0), stop=(c == 7)), wait=wmm)
        tsq_prev[cc % 2] = tsq_mm[3]
    sq_ev = [V(lambda e, gg=g: e.tensor_copy(
        out=orow[:, 512 * gg:512 * gg + 512], in_=psrow[gg][0:1, 0:512]),
        wait=[tsq_mm[g], tr_ev]) for g in range(4)]
    tr2_ev = None
    for sl in range(16):
        w = [sq_ev[3]] if sl == 0 else [tr2_ev]
        tt = PE(lambda e, s=sl: e.transpose(
            out=PS[:, 66 + (s % 2):67 + (s % 2)],
            in_=orow[:, 128 * s:128 * (s + 1)],
            identity=c_idf[0:1, 0:1]), wait=w)
        tr2_ev = V(lambda e, s=sl: e.tensor_copy(
            out=ostat[:, 16 + s:17 + s], in_=PS[:, 66 + (s % 2):67 + (s % 2)]),
            wait=[tt])
    V(lambda e: e.tensor_scalar(out=ostat[:, 32:48], in0=ostat[:, 0:16],
                                scalar1=1.0 / C, scalar2=None, op0=ALU.mult),
      wait=[tr2_ev], flush=True)
    V(lambda e: e.tensor_scalar(out=ostat[:, 0:16], in0=ostat[:, 16:32],
                                scalar1=1.0 / C, scalar2=None, op0=ALU.mult),
      flush=True)
    V(lambda e: e.tensor_tensor(out=ostat[:, 16:32], in0=ostat[:, 32:48],
                                in1=ostat[:, 32:48], op=ALU.mult), flush=True)
    V(lambda e: e.tensor_tensor(out=ostat[:, 0:16], in0=ostat[:, 0:16],
                                in1=ostat[:, 16:32], op=ALU.subtract), flush=True)
    tve = V(lambda e: e.tensor_scalar(out=ostat[:, 0:16], in0=ostat[:, 0:16],
                                      scalar1=1e-5, scalar2=None, op0=ALU.add),
            flush=True)
    tln = SCE(lambda e: e.activation(out=ostat[:, 16:32], in_=ostat[:, 0:16],
                                     func=AF.Ln), wait=[tve])
    trr = SCE(lambda e: e.activation(out=ostat[:, 48:64], in_=ostat[:, 16:32],
                                     func=AF.Exp, scale=-0.5), flush=True)
    tb2a = V(lambda e: e.tensor_tensor(out=ostat[:, 64:80], in0=ostat[:, 48:64],
                                       in1=ostat[:, 32:48], op=ALU.mult),
             wait=[trr], flush=True)
    tb2b = V(lambda e: e.tensor_scalar(out=ostat[:, 64:80], in0=ostat[:, 64:80],
                                       scalar1=-1.0, scalar2=None, op0=ALU.mult),
             flush=True)
    tap("ostat", ostat[:, 0:96])

    if last_phase <= 55:
        return finish_early()

    Wo = arb(32, 8192)
    Wp = arb(48, 8192)
    wst6 = [arf(64, 1024), arf(68, 1024)]
    gt6 = [arb(72, 1024), arb(73, 1024)]
    xt6 = [arf(74, 1024), arf(78, 1024)]
    ep1 = [arf(82, 512), arf(84, 512)]
    ep2 = [arf(86, 512), arf(88, 512)]
    outst = [arf(90, 1024), arf(94, 1024)]

    wo_cast = []
    for cc in range(8):
        w = [("PE", S.counts["PE"]), ("DV", S.counts["DV"])] if cc == 0 else \
            ([wo_cast[cc - 2]] if cc >= 2 else [])
        tl = dma("sync", "QW6a" if cc % 2 == 0 else "QW6b", wst6[cc % 2],
                 WO[128 * cc:128 * (cc + 1), :], wait=w)
        tc = V(lambda e, c=cc: e.tensor_copy(out=Wo[:, 1024 * c:1024 * (c + 1)],
                                             in_=wst6[c % 2]), wait=[tl])
        wo_cast.append(tc)
        wp_tok = V(lambda e, c=cc: e.tensor_scalar(
            out=Wp[:, 1024 * c:1024 * (c + 1)], in0=Wo[:, 1024 * c:1024 * (c + 1)],
            scalar1=lngb[:, c:c + 1], scalar2=None, op0=ALU.mult), flush=True)
        wo_cast[-1] = wp_tok
    tu_mm = [None, None]
    tc_mm = [None, None]
    for cc in range(8):
        for j in range(2):
            tu_mm[j] = PE(lambda e, c=cc, jj=j: e.matmul(
                out=(PA0 if jj == 0 else PA1)[0:1, 0:512], lhsT=c_one[:],
                rhs=Wp[:, 1024 * c + 512 * jj:1024 * c + 512 * (jj + 1)],
                start=(c == 0), stop=(c == 7)),
                wait=[wo_cast[-1]] if cc == 0 and j == 0 else [])
            tc_mm[j] = PE(lambda e, c=cc, jj=j: e.matmul(
                out=(PB0 if jj == 0 else PB1)[0:1, 0:512], lhsT=lnbb[:, c:c + 1],
                rhs=Wo[:, 1024 * c + 512 * jj:1024 * c + 512 * (jj + 1)],
                start=(c == 0), stop=(c == 7)))
    tbo = dma("sync", "QW", bo_row[:, :], BO[None, :], wait=["DV"])
    tue = [V(lambda e, jj=j: e.tensor_copy(out=orow[:, 512 * jj:512 * (jj + 1)],
                                           in_=(PA0 if jj == 0 else PA1)[0:1, 0:512]),
             wait=[tu_mm[j]]) for j in range(2)]
    tce = [V(lambda e, jj=j: e.tensor_tensor(
        out=orow[:, 1024 + 512 * jj:1024 + 512 * (jj + 1)],
        in0=(PB0 if jj == 0 else PB1)[0:1, 0:512],
        in1=bo_row[:, 512 * jj:512 * (jj + 1)], op=ALU.add),
        wait=[tc_mm[j], tbo]) for j in range(2)]
    tdu = dma("sync", "QS", d_u[None, :], orow[:, 0:1024], wait=tue)
    tdc = dma("sync", "QS", d_c0[None, :], orow[:, 1024:2048], wait=tce)
    tub = dma("sync", "QS", ubc[:], d_u[None, :].to_broadcast([128, 1024]),
              wait=[tdu])
    tcb = dma("sync", "QS", c0bc[:], d_c0[None, :].to_broadcast([128, 1024]),
              wait=[tdc, tub])

    gt_w = [None, None]
    xt_w = [None, None]
    ep_war = [tue[0], tue[1]]
    out_war = [None, None]
    final_tok = None
    for t in range(NT_OWN):
        bb = t % 2
        tgl = dma("sync", "QD0" if bb == 0 else "QD1", gt6[bb],
                  d_g[128 * t:128 * (t + 1), :],
                  wait=(gt_w[bb] or [("PE", S.counts["PE"]),
                                     ("QD0", S.counts["QD0"]),
                                     ("QD1", S.counts["QD1"])]))
        txl = dma("sync", "QX0" if bb == 0 else "QX1", xt6[bb],
                  X[128 * t:128 * (t + 1), :],
                  wait=(xt_w[bb] or [("PE", S.counts["PE"]),
                                     ("QX0", S.counts["QX0"]),
                                     ("QX1", S.counts["QX1"])]))
        ep_toks = []
        for j in range(2):
            pj = (2 * t + j) % 2
            tmm = None
            for cc in range(8):
                w = [wo_cast[-1]]
                if cc == 0 and ep_war[pj] is not None:
                    w.append(ep_war[pj])
                tmm = PE(lambda e, c=cc, jj=j, p=pj, tt=t: e.matmul(
                    out=(PA0 if p == 0 else PA1)[:],
                    lhsT=OC_v[:, c, 128 * tt:128 * (tt + 1)],
                    rhs=Wp[:, 1024 * c + 512 * jj:1024 * c + 512 * (jj + 1)],
                    start=(c == 0), stop=(c == 7)),
                    wait=w)
            te1 = V(lambda e, p=pj, tt=t: e.tensor_scalar(
                out=ep1[p], in0=(PA0 if p == 0 else PA1)[:],
                scalar1=ostat[:, 48 + tt:49 + tt], scalar2=None, op0=ALU.mult),
                wait=[tmm, tb2b])
            ep_war[pj] = te1
            te2 = V(lambda e, p=pj, tt=t, jj=j: e.scalar_tensor_tensor(
                out=ep2[p], in0=ubc[:, 512 * jj:512 * (jj + 1)],
                scalar=ostat[:, 64 + tt:65 + tt], in1=ep1[p],
                op0=ALU.mult, op1=ALU.add), wait=[tub, te1])
            te3 = V(lambda e, p=pj, jj=j: e.tensor_tensor(
                out=ep1[p], in0=ep2[p], in1=c0bc[:, 512 * jj:512 * (jj + 1)],
                op=ALU.add), wait=[tcb, te2])
            te4 = V(lambda e, p=pj, jj=j, b2=bb: e.tensor_tensor(
                out=ep2[p], in0=ep1[p], in1=gt6[b2][:, 512 * jj:512 * (jj + 1)],
                op=ALU.mult), wait=[tgl, te3])
            te5 = V(lambda e, p=pj, jj=j, b2=bb: e.tensor_tensor(
                out=outst[b2][:, 512 * jj:512 * (jj + 1)], in0=ep2[p],
                in1=xt6[b2][:, 512 * jj:512 * (jj + 1)], op=ALU.add),
                wait=[txl, te4] + ([out_war[bb]] if (j == 0 and out_war[bb]) else []),
                flush=True)
            ep_toks.append(te5)
        tso = dma("sync", "QO", OUT[128 * t:128 * (t + 1), :], outst[bb],
                  wait=[ep_toks[-1]])
        out_war[bb] = tso
        gt_w[bb] = [ep_toks[-1]]
        xt_w[bb] = [ep_toks[-1]]
        final_tok = tso

    S.waitonly("sync", [(s, S.counts[s]) for s in S.counts
                        if S.owner[s] == "" and S.counts[s] > 0])
    emit()
    ctx.close()
    return nc, S.ninstr


_BUILT = None


def kernel(L0_attn_x, L0_attn_W_q1, L0_attn_b_q1, L0_attn_W_q2, L0_attn_b_q2,
           L0_attn_W_k1, L0_attn_b_k1, L0_attn_W_k2, L0_attn_b_k2,
           L0_attn_W_v, L0_attn_b_v, L0_attn_W_out, L0_attn_b_out,
           L0_attn_ln_g, L0_attn_ln_b, L0_ffn_router_w, L0_ffn_router_b,
           L0_ffn_w1, L0_ffn_b1, L0_ffn_w2, L0_ffn_b2):
    global _BUILT
    x = np.ascontiguousarray(np.asarray(L0_attn_x, dtype=np.float32)
                             .reshape(T_ALL, C))
    bias6 = np.zeros((8, C), np.float32)
    for i, b in enumerate([L0_attn_b_q1, L0_attn_b_q2, L0_attn_b_k1,
                           L0_attn_b_k2, L0_attn_b_v, L0_attn_ln_g,
                           L0_attn_ln_b]):
        bias6[i] = np.asarray(b, dtype=np.float32)
    cidn = np.eye(128, dtype=np.float32)
    cslt = np.tril(np.ones((128, 128), np.float32), -1)
    cslt = np.triu(np.ones((128, 128), np.float32), 1)

    common = dict(
        WQ1=np.asarray(L0_attn_W_q1, np.float32),
        WQ2=np.asarray(L0_attn_W_q2, np.float32),
        WK1=np.asarray(L0_attn_W_k1, np.float32),
        WK2=np.asarray(L0_attn_W_k2, np.float32),
        WV=np.asarray(L0_attn_W_v, np.float32),
        WO=np.asarray(L0_attn_W_out, np.float32),
        BIAS6=bias6,
        BO=np.asarray(L0_attn_b_out, np.float32),
        WR=np.asarray(L0_ffn_router_w, np.float32),
        BR=np.asarray(L0_ffn_router_b, np.float32),
        FW1=np.asarray(L0_ffn_w1, np.float32),
        FB1=np.asarray(L0_ffn_b1, np.float32),
        FW2=np.asarray(L0_ffn_w2, np.float32),
        FB2=np.asarray(L0_ffn_b2, np.float32),
        CIDN=cidn, CSLT=cslt,
    )
    in_maps = []
    for c in range(8):
        b, hh = c // 2, c % 2
        own0 = b * T_BATCH + hh * T_OWN
        sib0 = b * T_BATCH + (1 - hh) * T_OWN
        own = x[own0:own0 + T_OWN]
        sib = x[sib0:sib0 + T_OWN]
        rest = np.concatenate([x[0:b * T_BATCH], x[(b + 1) * T_BATCH:]], axis=0)
        xp = np.ascontiguousarray(np.concatenate([own, sib, rest], axis=0))
        m = dict(common)
        m["X"] = xp
        in_maps.append(m)

    nc, _ = build()
    res = run_bass_kernel_spmd(nc, in_maps, core_ids=list(range(8)))
    out = np.empty((4, 4096, C), np.float32)
    for c in range(8):
        b, hh = c // 2, c % 2
        out[b, hh * T_OWN:(hh + 1) * T_OWN, :] = res.results[c]["OUT"]
    return out
